# version 03 (final)
# baseline (speedup 1.0000x reference)
import numpy as np
import concourse.bass as bass
import concourse.mybir as mybir
from concourse.bass_utils import run_bass_kernel_spmd

F32 = mybir.dt.float32
BF16 = mybir.dt.bfloat16
OP = mybir.AluOpType
AX = mybir.AxisListType
ACT = mybir.ActivationFunctionType

NCORES = 8
T = 2048
C = 1024
H = 4096
E = 2048
NQ = 8
NT = 4
TW = 512
EPS = 1e-5

DBG = False
UPTO = ""

CV = ["ln1g", "ln1b", "ln2g", "ln2b", "decay", "tfirst",
      "tmk", "tmv", "tmr", "cmk", "cmr"]


class Sched:
    ENGS = ("tensor", "vector", "scalar", "sync", "gpsimd")
    OWN = {"tensor": "T", "vector": "V", "scalar": "S", "sync": "QN", "gpsimd": "GN"}
    XTRA = {"sync": "Q", "gpsimd": "G"}

    def __init__(self, sems):
        self.sems = sems
        self.cnt = {k: 0 for k in sems}
        self.q = {e: [] for e in self.ENGS}
        self.bank = {}

    def emit(self, eng, fn, mark=None, n=1):
        if mark is not None:
            self.cnt[mark] += n
            sem = self.sems[mark]
            self.q[eng].append(lambda e, fn=fn, s=sem, k=n: fn(e).then_inc(s, k))
            return self.cnt[mark]
        self.q[eng].append(fn)
        return None

    def wait(self, eng, sem_name, value=None):
        v = self.cnt[sem_name] if value is None else value
        if v <= 0:
            return
        sem = self.sems[sem_name]
        self.q[eng].append(lambda e, s=sem, vv=v: e.wait_ge(s, vv))

    def claim_bank(self, b):
        if b in self.bank:
            sem, cntv = self.bank[b]
            self.wait("tensor", sem, cntv)
            del self.bank[b]

    def consume_bank(self, b, sem_name, cntv):
        self.bank[b] = (sem_name, cntv)

    def barrier(self, engs=None):
        engs = engs or self.ENGS
        for e in engs:
            if e == "tensor":
                self.wait(e, "T", self.cnt["T"])
            self.emit(e, lambda x: x.nop(), mark=self.OWN[e])
        names = [self.OWN[e] for e in engs] + [self.XTRA[e] for e in engs
                                               if e in self.XTRA]
        snap = {nm: self.cnt[nm] for nm in names}
        for e in engs:
            for nm in names:
                if nm != self.OWN[e]:
                    self.wait(e, nm, snap[nm])
        self.bank = {}


def ldq(ap, p=128):
    return ap.rearrange("(q p) w -> p q w", p=p)


def wch(w, rq, cq):
    return w[:, rq * 1024 + cq * 128: rq * 1024 + (cq + 1) * 128]


def build_program():
    nc = bass.Bass("TRN2", target_bir_lowering=False, debug=False,
                   detect_race_conditions=False)
    dk = "ExternalOutput" if DBG else "Internal"

    x_fm = nc.dram_tensor("x_fm", [C, T + 1], F32, kind="ExternalInput").ap()
    out_fm = nc.dram_tensor("out_fm", [C, T], F32, kind="ExternalOutput").ap()
    wkT = nc.dram_tensor("wkT", [C, C], F32, kind="ExternalInput").ap()
    wvT = nc.dram_tensor("wvT", [C, C], F32, kind="ExternalInput").ap()
    wrT = nc.dram_tensor("wrT", [C, C], F32, kind="ExternalInput").ap()
    woT = nc.dram_tensor("woT", [C, C], F32, kind="ExternalInput").ap()
    fwkT = nc.dram_tensor("fwkT", [C, H], F32, kind="ExternalInput").ap()
    fwvT = nc.dram_tensor("fwvT", [H, C], F32, kind="ExternalInput").ap()
    fwrT = nc.dram_tensor("fwrT", [C, C], F32, kind="ExternalInput").ap()
    phiT = nc.dram_tensor("phiT", [C, E], F32, kind="ExternalInput").ap()
    phib = nc.dram_tensor("phib", [1, E], F32, kind="ExternalInput").ap()
    expw = nc.dram_tensor("expw", [8, C, C], F32, kind="ExternalInput").ap()
    expb = nc.dram_tensor("expb", [8, C], F32, kind="ExternalInput").ap()
    cv_in = nc.dram_tensor("cv", [128, 8 * len(CV)], F32, kind="ExternalInput").ap()
    oc_in = nc.dram_tensor("oc", [128, 1], F32, kind="ExternalInput").ap()
    id_in = nc.dram_tensor("ident", [128, 128], F32, kind="ExternalInput").ap()
    cmul_in = nc.dram_tensor("cmul", [128, 8], F32, kind="ExternalInput").ap()
    cmw_in = nc.dram_tensor("cmw", [128, 128], F32, kind="ExternalInput").ap()
    selp_in = nc.dram_tensor("selp", [128, 64], F32, kind="ExternalInput").ap()
    mask0_in = nc.dram_tensor("mask0", [128, 1], F32, kind="ExternalInput").ap()
    xb_in = nc.dram_tensor("xb", [128, 8], F32, kind="ExternalInput").ap()

    sp_xin = nc.dram_tensor("sp_xin", [C, T], F32, kind=dk).ap()
    sp_ek = nc.dram_tensor("sp_ek", [C, T], F32, kind=dk).ap()
    sp_evv = nc.dram_tensor("sp_evv", [C, T], F32, kind=dk).ap()
    sp_sigr = nc.dram_tensor("sp_sigr", [C, T], F32, kind=dk).ap()
    sp_xmid = nc.dram_tensor("sp_xmid", [C, T], F32, kind=dk).ap()
    sp_sig2 = nc.dram_tensor("sp_sig2", [C, T], F32, kind=dk).ap()
    sp_x1 = nc.dram_tensor("sp_x1", [C, T], F32, kind=dk).ap()
    sp_kv = nc.dram_tensor("sp_kv", [C, T], F32, kind=dk).ap()
    sp_zfh = nc.dram_tensor("sp_zfh", [C, T], BF16, kind=dk).ap()
    sp_zfl = nc.dram_tensor("sp_zfl", [C, T], BF16, kind=dk).ap()
    sp_zth = nc.dram_tensor("sp_zth", [T, C], BF16, kind=dk).ap()
    sp_ztl = nc.dram_tensor("sp_ztl", [T, C], BF16, kind=dk).ap()
    sp_csh = nc.dram_tensor("sp_csh", [E, T], BF16, kind=dk).ap()
    sp_csl = nc.dram_tensor("sp_csl", [E, T], BF16, kind=dk).ap()
    sp_xa = nc.dram_tensor("sp_xa", [NT * 128, 4104], F32, kind="Internal").ap()
    sp_rows = nc.dram_tensor("sp_rows", [32, TW], F32, kind="Internal").ap()
    sp_row1 = nc.dram_tensor("sp_row1", [8, 8], F32, kind="Internal").ap()
    sp_wneg = nc.dram_tensor("sp_wneg", [128, 8], F32, kind="Internal").ap()
    sp_bnd = nc.dram_tensor("sp_bnd", [128, 8], F32, kind="Internal").ap()
    sp_bnd64 = nc.dram_tensor("sp_bnd64", [128, 1024], F32, kind="Internal").ap()
    sp_b2 = nc.dram_tensor("sp_b2", [128, 8], F32, kind="Internal").ap()

    dbg4 = nc.dram_tensor("dbg4", [128, 16384], F32,
                          kind="ExternalOutput" if DBG else "Internal").ap()
    dbg5 = nc.dram_tensor("dbg5", [128, 64], F32,
                          kind="ExternalOutput" if DBG else "Internal").ap()
    dbg6 = nc.dram_tensor("dbg6", [1, 64], F32,
                          kind="ExternalOutput" if DBG else "Internal").ap()
    cc1_in = nc.dram_tensor("cc1_in", [128, 16], F32)
    cc1_out = nc.dram_tensor("cc1_out", [NCORES * 128, 16], F32)
    dbg7 = nc.dram_tensor("dbg7", [128, 160], F32, kind=dk).ap()
    dbgx1 = nc.dram_tensor("dbgx1", [2048, 1024], F32, kind=dk).ap()
    dbgx2 = nc.dram_tensor("dbgx2", [256, 1024], F32, kind=dk).ap()
    dbgy1 = nc.dram_tensor("dbgy1", [256, 1024], F32, kind=dk).ap()
    dbgy2 = nc.dram_tensor("dbgy2", [2048, 1024], F32, kind=dk).ap()
    cc2_in = nc.dram_tensor("cc2_in", [128, 8], F32)
    cc2_out = nc.dram_tensor("cc2_out", [NCORES * 128, 8], F32)
    ccx_in = nc.dram_tensor("ccx_in", [E, C], F32)
    ccx_out = nc.dram_tensor("ccx_out", [E // NCORES, C], F32)
    ccy_in = nc.dram_tensor("ccy_in", [E // NCORES, C], F32)
    ccy_out = nc.dram_tensor("ccy_out", [E, C], F32)

    from contextlib import ExitStack
    from contextlib import ExitStack as ES
    with ExitStack() as top:
        en = top.enter_context
        sems = {k: en(nc.semaphore(k)) for k in ("T", "V", "S", "Q", "G", "CC", "QN", "GN")}
        sch = Sched(sems)

        def vflush(n=3, buf=None):
            b = wrep if buf is None else buf
            for _ in range(n):
                sch.emit("vector", lambda e, b=b: e.memset(b[:, :], 0.0),
                         mark="V")

        def sflush(n=3, buf=None):
            b = wrep if buf is None else buf
            for _ in range(n):
                sch.emit("scalar", lambda e, b=b: e.copy(b[:, 0:128],
                                                         b[:, 0:128]), mark="S")

        def finish():
            blk = en(nc.Block())
            blk.tensor(lambda e: [f(e) for f in sch.q["tensor"]])
            blk.vector(lambda e: [f(e) for f in sch.q["vector"]])
            blk.scalar(lambda e: [f(e) for f in sch.q["scalar"]])
            blk.sync(lambda e: [f(e) for f in sch.q["sync"]])
            blk.gpsimd(lambda e: [f(e) for f in sch.q["gpsimd"]])
            return nc
        pb = [en(nc.psum_tensor(f"pb{i}", [128, 512], F32)) for i in range(8)]

        cvt = en(nc.sbuf_tensor("cvt", [128, 8 * len(CV)], F32))
        ocv = en(nc.sbuf_tensor("ocv", [128, 1], F32))
        idt = en(nc.sbuf_tensor("idt", [128, 128], F32))
        cmt = en(nc.sbuf_tensor("cmt", [128, 8], F32))
        cmw = en(nc.sbuf_tensor("cmws", [128, 128], F32))
        wrep = en(nc.sbuf_tensor("wrep", [128, 128], F32))
        sel = en(nc.sbuf_tensor("sel", [128, 64], F32))
        msk = en(nc.sbuf_tensor("msk", [128, 1], F32))
        wneg = en(nc.sbuf_tensor("wneg", [128, 8], F32))
        dcol = en(nc.sbuf_tensor("dcol", [128, 8], F32))
        eucl = en(nc.sbuf_tensor("eucl", [128, 8], F32))
        stA = en(nc.sbuf_tensor("stA", [128, 8], F32))
        stB = en(nc.sbuf_tensor("stB", [128, 8], F32))
        carA = en(nc.sbuf_tensor("carA", [128, 8], F32))
        carB = en(nc.sbuf_tensor("carB", [128, 8], F32))
        keep = en(nc.sbuf_tensor("keep", [128, 8], F32))
        keep2 = en(nc.sbuf_tensor("keep2", [128, 8], F32))
        pubx = en(nc.sbuf_tensor("pubx", [128, 8], F32))
        rowb = en(nc.sbuf_tensor("rowb", [128, 2 * TW], F32))
        rsc = en(nc.sbuf_tensor("rsc", [1, 4 * TW], F32))
        tiny = en(nc.sbuf_tensor("tiny", [128, 64], F32))
        epsb = en(nc.sbuf_tensor("epsb", [128, 1], F32))
        gth = en(nc.sbuf_tensor("gth", [128, 128], F32))

        def cvc(name, q):
            i = CV.index(name)
            return cvt[:, 8 * i + q:8 * i + q + 1]

        def cva(name):
            i = CV.index(name)
            return cvt[:, 8 * CV.index(name):8 * CV.index(name) + 8]

        rows_used = [0]

        def ln_rows(src_of_q, sqbuf, skip_sq=False):
            sch.wait("scalar", "V")
            sch.wait("scalar", "Q")
            if not skip_sq:
                for q in range(NQ):
                    sch.emit("scalar", lambda e, q=q: e.activation(
                        sqbuf[:, q * TW:(q + 1) * TW], src_of_q(q), ACT.Square),
                        mark="S")
            sch.wait("tensor", "S")
            sch.wait("tensor", "V")
            sch.wait("tensor", "Q")
            sch.claim_bank(6)
            sch.claim_bank(7)
            for q in range(NQ):
                sch.emit("tensor", lambda e, q=q: e.matmul(
                    pb[6][0:1, 0:TW], ocv[:, :], src_of_q(q),
                    start=(q == 0), stop=(q == NQ - 1)), mark="T")
            for q in range(NQ):
                sch.emit("tensor", lambda e, q=q: e.matmul(
                    pb[7][0:1, 0:TW], ocv[:, :], sqbuf[:, q * TW:(q + 1) * TW],
                    start=(q == 0), stop=(q == NQ - 1)), mark="T")
            sch.wait("vector", "T")
            sch.emit("vector", lambda e: e.tensor_tensor(
                rsc[0:1, 3 * TW:4 * TW], pb[6][0:1, 0:TW], epsb[0:1, 0:1].broadcast_to([1, TW]),
                OP.bypass), mark="V")
            sch.emit("vector", lambda e: e.tensor_tensor(
                rsc[0:1, 2 * TW:3 * TW], rsc[0:1, 3 * TW:4 * TW],
                rsc[0:1, 3 * TW:4 * TW], OP.mult), mark="V")
            sch.emit("vector", lambda e: e.tensor_tensor(
                rsc[0:1, 2 * TW:3 * TW], pb[7][0:1, 0:TW], rsc[0:1, 2 * TW:3 * TW],
                OP.subtract), mark="V")
            sch.wait("scalar", "V")
            sch.emit("scalar", lambda e: e.activation(
                rsc[0:1, 2 * TW:3 * TW], rsc[0:1, 2 * TW:3 * TW], ACT.Ln,
                bias=epsb[0:1, 0:1]), mark="S")
            sch.emit("scalar", lambda e: e.activation(
                rsc[0:1, 0:TW], rsc[0:1, 2 * TW:3 * TW], ACT.Exp, scale=-0.5),
                mark="S")
            sch.wait("vector", "S")
            vm = sch.emit("vector", lambda e: e.tensor_tensor(
                rsc[0:1, TW:2 * TW], rsc[0:1, 3 * TW:4 * TW], rsc[0:1, 0:TW], OP.mult),
                mark="V")
            sch.consume_bank(6, "V", vm)
            sch.consume_bank(7, "V", vm)
            slot = rows_used[0]
            rows_used[0] = (rows_used[0] + 2) % 32
            sch.wait("sync", "V")
            sch.emit("sync", lambda e, slot=slot: e.dma_start(
                out=sp_rows[slot:slot + 1, :], in_=rsc[0:1, 0:TW]), mark="Q", n=16)
            sch.emit("sync", lambda e, slot=slot: e.dma_start(
                out=sp_rows[slot + 1:slot + 2, :], in_=rsc[0:1, TW:2 * TW]),
                mark="Q", n=16)
            sch.wait("sync", "Q")
            sch.emit("sync", lambda e, slot=slot: e.dma_start(
                out=rowb[:, 0:TW],
                in_=sp_rows[slot:slot + 1, :].broadcast_to([128, TW])), mark="Q", n=16)
            sch.emit("sync", lambda e, slot=slot: e.dma_start(
                out=rowb[:, TW:2 * TW],
                in_=sp_rows[slot + 1:slot + 2, :].broadcast_to([128, TW])),
                mark="Q", n=16)
            sch.wait("vector", "Q")

        def ln_apply(src, dst, q, affine=None, add=None):
            sch.emit("vector", lambda e: e.tensor_tensor(
                dst, src, rowb[:, 0:TW], OP.mult), mark="V")
            sch.emit("vector", lambda e: e.tensor_tensor(
                dst, dst, rowb[:, TW:2 * TW], OP.subtract), mark="V")
            if affine is not None:
                g, b = affine
                sch.emit("vector", lambda e: e.tensor_scalar(
                    dst, dst, cvc(g, q), cvc(b, q), OP.mult, OP.add), mark="V")
            if add is not None:
                sch.emit("vector", lambda e: e.tensor_tensor(
                    dst, dst, add, OP.add), mark="V")

        def mm3(bank, lh, ll, rh, rl, start, stop, n=TW, m=128, mark=False):
            if start:
                sch.claim_bank(bank)
            sch.emit("tensor", lambda e: e.matmul(
                pb[bank][0:m, 0:n], lh, rh, start=start, stop=False))
            sch.emit("tensor", lambda e: e.matmul(
                pb[bank][0:m, 0:n], lh, rl, start=False, stop=False))
            sch.emit("tensor", lambda e: e.matmul(
                pb[bank][0:m, 0:n], ll, rh, start=False, stop=stop),
                mark="T" if mark else None)

        def pair_split(src32, hi, lo):
            sch.emit("vector", lambda e: e.tensor_copy(hi, src32), mark="V")
            sch.emit("vector", lambda e: e.tensor_tensor(lo, src32, hi, OP.subtract),
                     mark="V")

        def load_weight_pair(wd, whi, wlo, stg, ncols, use_gpsimd=True):
            half = ncols // 2
            full = ldq(wd)
            hq = full.shape[1] // 2
            for h in range(2):
                sch.wait("sync", "V")
                sch.emit("sync", lambda e, h=h: e.dma_start(
                    out=stg[:, 0:half], in_=full[:, h * hq:(h + 1) * hq, :]),
                    mark="Q", n=16)
                if use_gpsimd:
                    sch.wait("gpsimd", "V")
                    sch.emit("gpsimd", lambda e, h=h: e.dma_start(
                        out=whi[:, h * half:(h + 1) * half],
                        in_=full[:, h * hq:(h + 1) * hq, :]), mark="G", n=16)
                    sch.wait("vector", "G")
                sch.wait("vector", "Q")
                step = 1024
                for q in range(half // step):
                    if not use_gpsimd:
                        sch.emit("vector", lambda e, q=q, h=h: e.tensor_copy(
                            whi[:, h * half + q * step:h * half + (q + 1) * step],
                            stg[:, q * step:(q + 1) * step]), mark="V")
                    sch.emit("vector", lambda e, q=q, h=h: e.tensor_tensor(
                        wlo[:, h * half + q * step:h * half + (q + 1) * step],
                        stg[:, q * step:(q + 1) * step],
                        whi[:, h * half + q * step:h * half + (q + 1) * step],
                        OP.subtract), mark="V")

        _mlc = [0]

        def _mck(tag):
            if UPTO == tag and _mlc[0] == 1:
                raise _FinishNow()

        class _FinishNow(Exception):
            pass

        def mini_ln(src_col, dst_col, affine):
            _mlc[0] += 1
            sch.wait("tensor", "V")
            sch.wait("tensor", "S")
            sch.wait("tensor", "Q")
            sch.claim_bank(6)
            sch.claim_bank(7)
            sch.emit("tensor", lambda e: e.matmul(
                pb[6][0:1, 0:8], ocv[:, :], src_col, start=True, stop=True), mark="T")
            sch.wait("scalar", "T")
            sch.emit("scalar", lambda e: e.activation(
                tiny[:, 32:40], src_col, ACT.Square), mark="S")
            sch.wait("tensor", "S")
            sch.emit("tensor", lambda e: e.matmul(
                pb[7][0:1, 0:8], ocv[:, :], tiny[:, 32:40], start=True, stop=True),
                mark="T")
            _mck("k_mm")
            sch.wait("vector", "T")
            sch.emit("vector", lambda e: e.reduce_sum(
                rsc[0:1, 0:1], pb[6][0:1, 0:8], AX.X), mark="V")
            vm = sch.emit("vector", lambda e: e.reduce_sum(
                rsc[0:1, 1:2], pb[7][0:1, 0:8], AX.X), mark="V")
            sch.consume_bank(6, "V", vm)
            sch.consume_bank(7, "V", vm)
            sch.emit("vector", lambda e: e.tensor_tensor(
                rsc[0:1, 2:3], rsc[0:1, 0:1], rsc[0:1, 0:1], OP.mult), mark="V")
            sch.emit("vector", lambda e: e.tensor_tensor(
                rsc[0:1, 2:3], rsc[0:1, 1:2], rsc[0:1, 2:3], OP.subtract), mark="V")
            _mck("k_red")
            sch.wait("scalar", "V")
            sch.emit("scalar", lambda e: e.activation(
                rsc[0:1, 3:4], rsc[0:1, 2:3], ACT.Ln, bias=epsb[0:1, 0:1]), mark="S")
            sch.emit("scalar", lambda e: e.activation(
                rsc[0:1, 4:5], rsc[0:1, 3:4], ACT.Exp, scale=-0.5), mark="S")
            sch.wait("vector", "S")
            sch.emit("vector", lambda e: e.tensor_tensor(
                rsc[0:1, 5:6], rsc[0:1, 0:1], rsc[0:1, 4:5], OP.mult), mark="V")
            _mck("k_act")
            sch.wait("sync", "V")
            sch.emit("sync", lambda e: e.dma_start(
                out=sp_row1[0:1, 0:2], in_=rsc[0:1, 4:6]), mark="Q", n=16)
            sch.wait("sync", "Q")
            sch.emit("sync", lambda e: e.dma_start(
                out=tiny[:, 40:41],
                in_=sp_row1[0:1, 0:1].broadcast_to([128, 1])), mark="Q", n=16)
            sch.emit("sync", lambda e: e.dma_start(
                out=tiny[:, 41:42],
                in_=sp_row1[0:1, 1:2].broadcast_to([128, 1])), mark="Q", n=16)
            _mck("k_dma")
            sch.wait("vector", "Q")
            sch.emit("vector", lambda e: e.tensor_scalar(
                dst_col, src_col, tiny[:, 40:41], tiny[:, 41:42],
                OP.mult, OP.subtract), mark="V")
            if affine is not None:
                g, b = affine
                sch.emit("vector", lambda e: e.tensor_tensor(
                    dst_col, dst_col, cva(g), OP.mult), mark="V")
                sch.emit("vector", lambda e: e.tensor_tensor(
                    dst_col, dst_col, cva(b), OP.add), mark="V")

        def mini_ln2(src_col, dst_col, affine, mlt, sqb):
            sch.wait("vector", "Q")
            vm0 = sch.emit("vector", lambda e: e.memset(mlt[:, :], 0.0), mark="V")
            sch.emit("vector", lambda e: e.memset(sqb[:, :], 0.0), mark="V")
            sch.emit("vector", lambda e: e.tensor_copy(
                mlt[:, 0:512].rearrange("p (q w) -> p q w", w=64),
                src_col.unsqueeze(2).broadcast_to([128, 8, 64])), mark="V")
            sch.emit("vector", lambda e: e.tensor_tensor(
                mlt[:, 512:1024], mlt[:, 0:512], mlt[:, 0:512], OP.mult),
                mark="V")
            vm1 = sch.cnt["V"]
            sch.wait("sync", "V", vm1)
            sch.emit("sync", lambda e: e.dma_start(
                out=sp_bnd64, in_=mlt[:, 0:1024]), mark="Q", n=16)
            sch.wait("sync", "Q")
            sch.emit("sync", lambda e: e.dma_start(
                out=mlt[:, :].rearrange("p (q w) -> p q w", w=TW)[:, :, 0:64],
                in_=sp_bnd64[:, 0:512].rearrange("p (q w) -> p q w", w=64)),
                mark="Q", n=16)
            sch.emit("sync", lambda e: e.dma_start(
                out=sqb[:, :].rearrange("p (q w) -> p q w", w=TW)[:, :, 0:64],
                in_=sp_bnd64[:, 512:1024].rearrange("p (q w) -> p q w", w=64)),
                mark="Q", n=16)
            ln_rows(lambda q: mlt[:, q * TW:(q + 1) * TW], sqb, skip_sq=True)
            sch.emit("vector", lambda e: e.tensor_scalar(
                dst_col, src_col, rowb[:, 0:1], rowb[:, TW:TW + 1],
                OP.mult, OP.subtract), mark="V")
            if affine is not None:
                g, b = affine
                sch.emit("vector", lambda e: e.tensor_tensor(
                    dst_col, dst_col, cva(g), OP.mult), mark="V")
                sch.emit("vector", lambda e: e.tensor_tensor(
                    dst_col, dst_col, cva(b), OP.add), mark="V")

        for (dst, src) in ((cvt, cv_in), (ocv, oc_in), (idt, id_in),
                           (cmt, cmul_in), (sel, selp_in), (msk, mask0_in),
                           (cmw, cmw_in)):
            sch.emit("sync", lambda e, dst=dst, src=src: e.dma_start(
                out=dst[:, :], in_=src), mark="Q", n=16)
        sch.wait("scalar", "Q")
        sch.emit("scalar", lambda e: e.activation(wneg[:, :], cva("decay"), ACT.Exp),
                 mark="S")
        sch.emit("scalar", lambda e: e.activation(eucl[:, :], cva("tfirst"), ACT.Exp),
                 mark="S")
        sch.wait("vector", "S")
        sch.emit("vector", lambda e: e.tensor_scalar_mul(wneg[:, :], wneg[:, :], -1.0),
                 mark="V")
        sch.wait("scalar", "V")
        sch.emit("scalar", lambda e: e.activation(dcol[:, :], wneg[:, :], ACT.Exp),
                 mark="S")
        for b in (stA, stB, carA, carB):
            sch.emit("vector", lambda e, b=b: e.memset(b[:, :], 0.0), mark="V")
        sch.emit("vector", lambda e: e.memset(epsb[:, :], float(EPS)), mark="V")
        sch.barrier()

        sch.emit("sync", lambda e: e.dma_start(
            out=tiny[:, 0:8], in_=xb_in), mark="Q", n=16)
        sch.wait("vector", "Q")
        sch.wait("scalar", "Q")
        if UPTO == "setup":
            return finish()
        with ES() as phk:
            mlt = phk.enter_context(nc.sbuf_tensor("mlt", [128, 4096], F32))
            sqk = phk.enter_context(nc.sbuf_tensor("sqk", [128, 4096], F32))
            mini_ln2(tiny[:, 0:8], tiny[:, 8:16], None, mlt, sqk)
            mini_ln2(tiny[:, 8:16], keep[:, :], ("ln1g", "ln1b"), mlt, sqk)
        sch.emit("vector", lambda e: e.tensor_scalar_mul(
            keep[:, :], keep[:, :], msk[:, 0:1]), mark="V")
        if DBG:
            vmk = sch.cnt["V"]
            sch.wait("sync", "V", vmk)
            sch.emit("sync", lambda e: e.dma_start(
                out=dbg5[:, 32:40], in_=keep[:, :]), mark="Q", n=16)
            sch.emit("sync", lambda e: e.dma_start(
                out=dbg5[:, 40:48], in_=tiny[:, 8:16]), mark="Q", n=16)
            sch.emit("sync", lambda e: e.dma_start(
                out=dbg5[:, 48:56], in_=rowb[:, 0:8]), mark="Q", n=16)
            sch.emit("sync", lambda e: e.dma_start(
                out=dbg5[:, 56:64], in_=rowb[:, TW:TW + 8]), mark="Q", n=16)
            sch.wait("vector", "Q")
        sch.barrier()
        if UPTO == "keeper":
            return finish()

        from contextlib import ExitStack as ES

        with ES() as ph:
            e1 = ph.enter_context
            wk_hi = e1(nc.sbuf_tensor("wk_hi", [128, 8192], BF16))
            wk_lo = e1(nc.sbuf_tensor("wk_lo", [128, 8192], BF16))
            wv_hi = e1(nc.sbuf_tensor("wv_hi", [128, 8192], BF16))
            wv_lo = e1(nc.sbuf_tensor("wv_lo", [128, 8192], BF16))
            wstg = e1(nc.sbuf_tensor("wstg", [128, 4096], F32))
            xt = e1(nc.sbuf_tensor("xt", [128, 4096], F32))
            sqb = e1(nc.sbuf_tensor("sqb", [128, 4096], F32))
            xint = e1(nc.sbuf_tensor("xint", [128, 4096], F32))
            xat = e1(nc.sbuf_tensor("xat", [128, 4104], F32))
            mx32 = sqb
            mxh = e1(nc.sbuf_tensor("mxh", [128, 4096], BF16))
            mxl = e1(nc.sbuf_tensor("mxl", [128, 4096], BF16))
            ekt = e1(nc.sbuf_tensor("ekt", [128, 4096], F32))
            evvt = e1(nc.sbuf_tensor("evvt", [128, 4096], F32))

            load_weight_pair(wkT, wk_hi, wk_lo, wstg, 8192)
            load_weight_pair(wvT, wv_hi, wv_lo, wstg, 8192)
            if UPTO == "s_w":
                return finish()

            for t in range(NT):
                sch.wait("sync", "V")
                sch.wait("sync", "S")
                sch.emit("sync", lambda e, t=t: e.dma_start(
                    out=xt[:, :], in_=ldq(x_fm[:, 512 * t + 1: 512 * t + 513])),
                    mark="Q", n=16)
                sch.wait("vector", "Q")
                sch.wait("scalar", "Q")
                ln_rows(lambda q: xt[:, q * TW:(q + 1) * TW], sqb)
                for q in range(NQ):
                    ln_apply(xt[:, q * TW:(q + 1) * TW],
                             xint[:, q * TW:(q + 1) * TW], q)
                vm = sch.cnt["V"]
                sch.wait("sync", "V", vm)
                sch.emit("sync", lambda e, t=t: e.dma_start(
                    out=ldq(sp_xin[:, 512 * t:512 * (t + 1)]), in_=xint[:, :]),
                    mark="Q", n=16)
                if UPTO == "s_ln" and t == 0:
                    return finish()
                ln_rows(lambda q: xint[:, q * TW:(q + 1) * TW], sqb)
                for q in range(NQ):
                    sch.emit("vector", lambda e, q=q: e.tensor_copy(
                        xat[:, q * 513:q * 513 + 1], keep[:, q:q + 1]), mark="V")
                    ln_apply(xint[:, q * TW:(q + 1) * TW],
                             xat[:, q * 513 + 1:q * 513 + 513], q,
                             affine=("ln1g", "ln1b"))
                for q in range(NQ):
                    sch.emit("vector", lambda e, q=q: e.tensor_copy(
                        keep[:, q:q + 1], xat[:, q * 513 + 512:q * 513 + 513]),
                        mark="V")
                for q in range(NQ):
                    sch.emit("vector", lambda e, q=q: e.tensor_tensor(
                        xt[:, q * TW:(q + 1) * TW], xat[:, q * 513 + 1:q * 513 + 513],
                        xat[:, q * 513:q * 513 + 512], OP.subtract), mark="V")

                vm = sch.cnt["V"]
                sch.wait("sync", "V", vm)
                sch.emit("sync", lambda e, t=t: e.dma_start(
                    out=sp_xa[t * 128:(t + 1) * 128, :], in_=xat[:, :]),
                    mark="Q", n=16)
                if UPTO == "s_mx" and t == 0:
                    return finish()
                for wi, (tmname, whi, wlo) in enumerate(
                        (("tmk", wk_hi, wk_lo), ("tmv", wv_hi, wv_lo))):
                    sch.wait("vector", "T")
                    for q in range(NQ):
                        sch.emit("vector", lambda e, q=q, tm=tmname: e.scalar_tensor_tensor(
                            mx32[:, q * TW:(q + 1) * TW], xt[:, q * TW:(q + 1) * TW],
                            cvc(tm, q), xat[:, q * 513:q * 513 + 512],
                            OP.mult, OP.add), mark="V")
                        pair_split(mx32[:, q * TW:(q + 1) * TW],
                                   mxh[:, q * TW:(q + 1) * TW],
                                   mxl[:, q * TW:(q + 1) * TW])
                    vm = sch.cnt["V"]
                    sch.wait("tensor", "V", vm)
                    if UPTO == "s_p1" and t == 0 and wi == 0:
                        return finish()
                    for oq in range(NQ):
                        bank = oq % 2
                        for cq in range(NQ):
                            mm3(bank, wch(whi, cq, oq), wch(wlo, cq, oq),
                                mxh[:, cq * TW:(cq + 1) * TW],
                                mxl[:, cq * TW:(cq + 1) * TW],
                                start=(cq == 0), stop=(cq == NQ - 1),
                                mark=(cq == NQ - 1))
                        if UPTO == "s_p2" and t == 0 and wi == 0 and oq == 1:
                            return finish()
                        tm = sch.cnt["T"]
                        if wi == 0:
                            sch.wait("scalar", "T", tm)
                            sm = sch.emit("scalar", lambda e, oq=oq, bank=bank: e.activation(
                                ekt[:, oq * TW:(oq + 1) * TW], pb[bank][:, :],
                                ACT.Exp), mark="S")
                            sch.consume_bank(bank, "S", sm)
                        else:
                            sch.wait("vector", "T", tm)
                            sm = sch.emit("vector", lambda e, oq=oq, bank=bank: e.tensor_tensor(
                                evvt[:, oq * TW:(oq + 1) * TW],
                                ekt[:, oq * TW:(oq + 1) * TW], pb[bank][:, :],
                                OP.mult), mark="V")
                            sch.consume_bank(bank, "V", sm)
                    if UPTO == "s_p3" and t == 0 and wi == 0:
                        return finish()

                if UPTO == "s_proj" and t == 0:
                    return finish()
                sch.wait("vector", "S")
                for q in range(NQ):
                    sch.emit("vector", lambda e, q=q: e.tensor_tensor_scan(
                        sqb[:, q * TW:(q + 1) * TW],
                        dcol[:, q:q + 1].broadcast_to([128, TW]),
                        evvt[:, q * TW:(q + 1) * TW], stA[:, q:q + 1],
                        OP.mult, OP.add), mark="V")
                vflush()
                vm = sch.cnt["V"]
                sch.wait("scalar", "V", vm)
                for q in range(NQ):
                    sch.emit("scalar", lambda e, q=q: e.copy(
                        stA[:, q:q + 1], sqb[:, q * TW + TW - 1:q * TW + TW]),
                        mark="S")
                sflush()
                sm = sch.cnt["S"]
                sch.wait("vector", "S", sm)
                for q in range(NQ):
                    sch.emit("vector", lambda e, q=q: e.tensor_tensor_scan(
                        sqb[:, q * TW:(q + 1) * TW],
                        dcol[:, q:q + 1].broadcast_to([128, TW]),
                        ekt[:, q * TW:(q + 1) * TW], stB[:, q:q + 1],
                        OP.mult, OP.add), mark="V")
                vflush()
                vm = sch.cnt["V"]
                sch.wait("scalar", "V", vm)
                for q in range(NQ):
                    sch.emit("scalar", lambda e, q=q: e.copy(
                        stB[:, q:q + 1], sqb[:, q * TW + TW - 1:q * TW + TW]),
                        mark="S")
                sflush()
                sm = sch.cnt["S"]
                sch.wait("vector", "S", sm)

                vm = sch.cnt["V"]
                sm = sch.cnt["S"]
                sch.wait("sync", "V", vm)
                sch.wait("sync", "S", sm)
                for (buf, sp) in ((ekt, sp_ek), (evvt, sp_evv)):
                    sch.emit("sync", lambda e, buf=buf, sp=sp, t=t: e.dma_start(
                        out=ldq(sp[:, 512 * t:512 * (t + 1)]), in_=buf[:, :]),
                        mark="Q", n=16)
                if UPTO == "s_t0" and t == 0:
                    return finish()
            sch.barrier()

        if UPTO == "stage1":
            return finish()

        sflush()
        sch.wait("sync", "S")
        sch.emit("sync", lambda e: e.dma_start(out=cc1_in.ap()[:, 0:8], in_=stA[:, :]),
                 mark="Q", n=16)
        sch.emit("sync", lambda e: e.dma_start(out=cc1_in.ap()[:, 8:16], in_=stB[:, :]),
                 mark="Q", n=16)
        with ES() as ph:
            eC = ph.enter_context
            wr_hi = eC(nc.sbuf_tensor("wr_hi", [128, 8192], BF16))
            wr_lo = eC(nc.sbuf_tensor("wr_lo", [128, 8192], BF16))
            wstgC = eC(nc.sbuf_tensor("wstgC", [128, 4096], F32))
            xatC = eC(nc.sbuf_tensor("xatC", [128, 4104], F32))
            uC = eC(nc.sbuf_tensor("uC", [128, 4096], F32))
            mxC = eC(nc.sbuf_tensor("mxC", [128, 4096], F32))
            mxhC = eC(nc.sbuf_tensor("mxhC", [128, 4096], BF16))
            mxlC = eC(nc.sbuf_tensor("mxlC", [128, 4096], BF16))
            srtC = eC(nc.sbuf_tensor("srtC", [128, 4096], F32))
            load_weight_pair(wrT, wr_hi, wr_lo, wstgC, 8192, use_gpsimd=False)
            for t in range(NT):
                sch.wait("sync", "V")
                sch.wait("sync", "S")
                sch.emit("sync", lambda e, t=t: e.dma_start(
                    out=xatC[:, :], in_=sp_xa[t * 128:(t + 1) * 128, :]),
                    mark="Q", n=16)
                sch.wait("vector", "Q")
                for q in range(NQ):
                    sch.emit("vector", lambda e, q=q: e.tensor_tensor(
                        uC[:, q * TW:(q + 1) * TW],
                        xatC[:, q * 513 + 1:q * 513 + 513],
                        xatC[:, q * 513:q * 513 + 512], OP.subtract), mark="V")
                sch.wait("vector", "T")
                for q in range(NQ):
                    sch.emit("vector", lambda e, q=q: e.scalar_tensor_tensor(
                        mxC[:, q * TW:(q + 1) * TW], uC[:, q * TW:(q + 1) * TW],
                        cvc("tmr", q), xatC[:, q * 513:q * 513 + 512],
                        OP.mult, OP.add), mark="V")
                    pair_split(mxC[:, q * TW:(q + 1) * TW],
                               mxhC[:, q * TW:(q + 1) * TW],
                               mxlC[:, q * TW:(q + 1) * TW])
                vm = sch.cnt["V"]
                sch.wait("tensor", "V", vm)
                for oq in range(NQ):
                    bank = oq % 2
                    for cq in range(NQ):
                        mm3(bank, wch(wr_hi, cq, oq), wch(wr_lo, cq, oq),
                            mxhC[:, cq * TW:(cq + 1) * TW],
                            mxlC[:, cq * TW:(cq + 1) * TW],
                            start=(cq == 0), stop=(cq == NQ - 1),
                        mark=(cq == NQ - 1))
                    tm = sch.cnt["T"]
                    sch.wait("scalar", "T", tm)
                    sm = sch.emit("scalar", lambda e, oq=oq, bank=bank: e.activation(
                        srtC[:, oq * TW:(oq + 1) * TW], pb[bank][:, :],
                        ACT.Sigmoid), mark="S")
                    sch.consume_bank(bank, "S", sm)
                sm = sch.cnt["S"]
                sch.wait("sync", "S", sm)
                sch.emit("sync", lambda e, t=t: e.dma_start(
                    out=ldq(sp_sigr[:, 512 * t:512 * (t + 1)]), in_=srtC[:, :]),
                    mark="Q", n=16)
            sch.barrier()

        if UPTO == "c1":
            return finish()

        sch.wait("gpsimd", "Q")
        sch.wait("gpsimd", "V")
        sch.wait("gpsimd", "S")
        sch.emit("gpsimd", lambda e: e.collective_compute(
            "AllGather", OP.bypass, replica_groups=[list(range(NCORES))],
            ins=[cc1_in.ap().opt()], outs=[cc1_out.ap().opt()]), mark="CC")
        sch.wait("sync", "CC")
        sch.emit("sync", lambda e: e.dma_start(
            out=gth[:, :],
            in_=cc1_out.ap().rearrange("(j p) c -> p j c", p=128)), mark="Q", n=16)
        sch.wait("scalar", "Q")
        sch.wait("vector", "Q")
        if DBG:
            sch.wait("sync", "Q")
            sch.emit("sync", lambda e: e.dma_start(
                out=dbg7[:, 0:128], in_=gth[:, :]), mark="Q", n=16)
            sch.emit("sync", lambda e: e.dma_start(
                out=dbg7[:, 144:160],
                in_=cc1_out.ap()[384:512, :]), mark="Q", n=16)
            sch.emit("sync", lambda e: e.dma_start(
                out=dbg7[:, 128:136], in_=stA[:, :]), mark="Q", n=16)
            sch.emit("sync", lambda e: e.dma_start(
                out=dbg7[:, 136:144], in_=stB[:, :]), mark="Q", n=16)
        vm0 = sch.cnt["V"]
        sch.wait("sync", "V", vm0)
        sch.emit("sync", lambda e: e.dma_start(
            out=sp_wneg, in_=wneg[:, :]), mark="Q", n=16)
        sch.wait("sync", "Q")
        sch.emit("sync", lambda e: e.dma_start(
            out=wrep[:, :],
            in_=sp_wneg.unsqueeze(1).broadcast_to([128, 16, 8])), mark="Q", n=16)
        sch.wait("vector", "Q")
        sch.emit("vector", lambda e: e.tensor_tensor(
            wrep[:, :], wrep[:, :], cmw[:, :], OP.mult), mark="V")
        vflush(buf=cmw)
        vm = sch.cnt["V"]
        if DBG:
            sch.wait("sync", "V", vm)
            sch.emit("sync", lambda e: e.dma_start(
                out=dbg4[:, 8192:8320], in_=wrep[:, :]), mark="Q", n=16)
            sch.wait("scalar", "Q")
        sch.wait("scalar", "V", vm)
        sch.emit("scalar", lambda e: e.activation(
            wrep[:, :], wrep[:, :], ACT.Exp), mark="S")
        sflush(buf=cmw)
        sm = sch.cnt["S"]
        if DBG:
            sch.wait("sync", "S", sm)
            sch.emit("sync", lambda e: e.dma_start(
                out=dbg4[:, 8320:8448], in_=wrep[:, :]), mark="Q", n=16)
            sch.wait("vector", "Q")
        sch.wait("vector", "S", sm)
        sch.emit("vector", lambda e: e.tensor_tensor(
            wrep[:, :], wrep[:, :], gth[:, :], OP.mult), mark="V")
        if DBG:
            vmg = sch.cnt["V"]
            sch.wait("sync", "V", vmg)
            sch.emit("sync", lambda e: e.dma_start(
                out=dbg4[:, 8448:8576], in_=wrep[:, :]), mark="Q", n=16)
            sch.wait("vector", "Q")
        sch.emit("vector", lambda e: e.tensor_reduce(
            carA[:, :],
            wrep[:, :].rearrange("p (j c) -> p c j", c=16)[:, 0:8, :],
            AX.X, OP.add), mark="V")
        sch.emit("vector", lambda e: e.tensor_reduce(
            carB[:, :],
            wrep[:, :].rearrange("p (j c) -> p c j", c=16)[:, 8:16, :],
            AX.X, OP.add), mark="V")
        if DBG:
            vm = sch.cnt["V"]
            sch.wait("sync", "V", vm)
            sch.emit("sync", lambda e: e.dma_start(
                out=dbg5[:, 0:8], in_=carA[:, :]), mark="Q", n=16)
            sch.emit("sync", lambda e: e.dma_start(
                out=dbg5[:, 8:16], in_=carB[:, :]), mark="Q", n=16)
            sch.emit("sync", lambda e: e.dma_start(
                out=dbg4[:, 0:128], in_=gth[:, :]), mark="Q", n=16)
        sch.barrier()

        with ES() as ph:
            e2 = ph.enter_context
            wo_hi = e2(nc.sbuf_tensor("wo_hi", [128, 8192], BF16))
            wo_lo = e2(nc.sbuf_tensor("wo_lo", [128, 8192], BF16))
            wstg2v = e2(nc.sbuf_tensor("wstg2", [128, 4096], F32))
            l_ek = e2(nc.sbuf_tensor("l_ek", [128, 4096], F32))
            l_evv = e2(nc.sbuf_tensor("l_evv", [128, 4096], F32))
            l_sr = e2(nc.sbuf_tensor("l_sr", [128, 4096], F32))
            l_xin = e2(nc.sbuf_tensor("l_xin", [128, 4096], F32))
            numb = e2(nc.sbuf_tensor("numb", [128, 4096], F32))
            denb = e2(nc.sbuf_tensor("denb", [128, 4096], F32))
            abuf = e2(nc.sbuf_tensor("abuf", [128, 4104], F32))
            bbuf = e2(nc.sbuf_tensor("bbuf", [128, 4104], F32))
            rkh = e2(nc.sbuf_tensor("rkh", [128, 4096], BF16))
            rkl = e2(nc.sbuf_tensor("rkl", [128, 4096], BF16))
            xmt = numb

            load_weight_pair(woT, wo_hi, wo_lo, wstg2v, 8192)

            for t in range(NT):
                sch.wait("sync", "V")
                sch.wait("sync", "S")
                for (buf, sp) in ((l_ek, sp_ek), (l_evv, sp_evv),
                                  (l_sr, sp_sigr), (l_xin, sp_xin)):
                    sch.emit("sync", lambda e, buf=buf, sp=sp, t=t: e.dma_start(
                        out=buf[:, :], in_=ldq(sp[:, 512 * t:512 * (t + 1)])),
                        mark="Q", n=16)
                sch.wait("vector", "Q")
                sch.wait("scalar", "Q")
                sch.wait("vector", "S")
                for q in range(NQ):
                    sch.emit("vector", lambda e, q=q: e.tensor_copy(
                        abuf[:, q * 513:q * 513 + 1], carA[:, q:q + 1]), mark="V")
                    sch.emit("vector", lambda e, q=q: e.tensor_copy(
                        bbuf[:, q * 513:q * 513 + 1], carB[:, q:q + 1]), mark="V")
                for q in range(NQ):
                    sch.emit("vector", lambda e, q=q: e.tensor_tensor_scan(
                        abuf[:, q * 513 + 1:q * 513 + 513],
                        dcol[:, q:q + 1].broadcast_to([128, TW]),
                        l_evv[:, q * TW:(q + 1) * TW], carA[:, q:q + 1],
                        OP.mult, OP.add), mark="V")
                    sch.emit("vector", lambda e, q=q: e.tensor_tensor_scan(
                        bbuf[:, q * 513 + 1:q * 513 + 513],
                        dcol[:, q:q + 1].broadcast_to([128, TW]),
                        l_ek[:, q * TW:(q + 1) * TW], carB[:, q:q + 1],
                        OP.mult, OP.add), mark="V")
                vflush()
                vm = sch.cnt["V"]
                sch.wait("scalar", "V", vm)
                for q in range(NQ):
                    sch.emit("scalar", lambda e, q=q: e.copy(
                        carA[:, q:q + 1], abuf[:, q * 513 + 512:q * 513 + 513]),
                        mark="S")
                    sch.emit("scalar", lambda e, q=q: e.copy(
                        carB[:, q:q + 1], bbuf[:, q * 513 + 512:q * 513 + 513]),
                        mark="S")
                sflush()
                for q in range(NQ):
                    sch.emit("vector", lambda e, q=q: e.scalar_tensor_tensor(
                        numb[:, q * TW:(q + 1) * TW], l_evv[:, q * TW:(q + 1) * TW],
                        eucl[:, q:q + 1], abuf[:, q * 513:q * 513 + 512],
                        OP.mult, OP.add), mark="V")
                    sch.emit("vector", lambda e, q=q: e.scalar_tensor_tensor(
                        denb[:, q * TW:(q + 1) * TW], l_ek[:, q * TW:(q + 1) * TW],
                        eucl[:, q:q + 1], bbuf[:, q * 513:q * 513 + 512],
                        OP.mult, OP.add), mark="V")
                vm = sch.cnt["V"]
                if DBG and t == 0:
                    sch.wait("sync", "V", vm)
                    sch.emit("sync", lambda e: e.dma_start(
                        out=dbg4[:, 4096:8192], in_=denb[:, :]), mark="Q", n=16)
                    sch.emit("sync", lambda e: e.dma_start(
                        out=dbg4[:, 8192:12288], in_=numb[:, :]), mark="Q", n=16)
                    sch.emit("sync", lambda e: e.dma_start(
                        out=dbg4[:, 12288:16384], in_=bbuf[:, 0:4096]), mark="Q", n=16)
                    sch.wait("scalar", "Q")
                sch.wait("scalar", "V", vm)
                for q in range(NQ):
                    sch.emit("scalar", lambda e, q=q: e.activation(
                        l_ek[:, q * TW:(q + 1) * TW], denb[:, q * TW:(q + 1) * TW],
                        ACT.Ln), mark="S")
                    sch.emit("scalar", lambda e, q=q: e.activation(
                        denb[:, q * TW:(q + 1) * TW], l_ek[:, q * TW:(q + 1) * TW],
                        ACT.Exp, scale=-1.0), mark="S")
                sch.wait("vector", "S")
                for q in range(NQ):
                    sch.emit("vector", lambda e, q=q: e.tensor_tensor(
                        numb[:, q * TW:(q + 1) * TW], numb[:, q * TW:(q + 1) * TW],
                        denb[:, q * TW:(q + 1) * TW], OP.mult), mark="V")
                    sch.emit("vector", lambda e, q=q: e.tensor_tensor(
                        numb[:, q * TW:(q + 1) * TW], numb[:, q * TW:(q + 1) * TW],
                        l_sr[:, q * TW:(q + 1) * TW], OP.mult), mark="V")
                    pair_split(numb[:, q * TW:(q + 1) * TW],
                               rkh[:, q * TW:(q + 1) * TW],
                               rkl[:, q * TW:(q + 1) * TW])
                vm = sch.cnt["V"]
                sch.wait("tensor", "V", vm)
                for cq in range(NQ):
                    bank = cq % 2
                    for oq in range(NQ):
                        mm3(bank, wch(wo_hi, oq, cq), wch(wo_lo, oq, cq),
                            rkh[:, oq * TW:(oq + 1) * TW],
                            rkl[:, oq * TW:(oq + 1) * TW],
                            start=(oq == 0), stop=(oq == NQ - 1),
                        mark=(oq == NQ - 1))
                    tm = sch.cnt["T"]
                    sch.wait("vector", "T", tm)
                    vv = sch.emit("vector", lambda e, cq=cq, bank=bank: e.scalar_tensor_tensor(
                        xmt[:, cq * TW:(cq + 1) * TW], l_xin[:, cq * TW:(cq + 1) * TW],
                        0.0, pb[bank][:, :], OP.bypass, OP.add), mark="V")
                    sch.consume_bank(bank, "V", vv)
                vm = sch.cnt["V"]
                sch.wait("sync", "V", vm)
                sch.emit("sync", lambda e, t=t: e.dma_start(
                    out=ldq(sp_xmid[:, 512 * t:512 * (t + 1)]), in_=xmt[:, :]),
                    mark="Q", n=16)
                if t == NT - 1:
                    for q in range(NQ):
                        sch.emit("vector", lambda e, q=q: e.tensor_copy(
                            pubx[:, q:q + 1], xmt[:, q * TW + TW - 1:q * TW + TW]),
                            mark="V")
                    vflush()
            sch.barrier()

        if UPTO == "c2":
            return finish()

        sch.emit("sync", lambda e: e.dma_start(out=cc2_in.ap()[:, :], in_=pubx[:, :]),
                 mark="Q", n=16)
        sch.wait("gpsimd", "Q")
        sch.emit("gpsimd", lambda e: e.collective_compute(
            "AllGather", OP.bypass, replica_groups=[list(range(NCORES))],
            ins=[cc2_in.ap().opt()], outs=[cc2_out.ap().opt()]), mark="CC")
        sch.wait("sync", "CC")
        sch.emit("sync", lambda e: e.dma_start(
            out=gth[:, 0:64],
            in_=cc2_out.ap().rearrange("(j p) c -> p j c", p=128)), mark="Q", n=16)
        sch.wait("vector", "Q")
        if DBG:
            sch.wait("sync", "Q")
            sch.emit("sync", lambda e: e.dma_start(
                out=dbg4[:, 256:320], in_=gth[:, 0:64]), mark="Q", n=16)
        sch.emit("vector", lambda e: e.memset(wrep[:, :], 0.0), mark="V")
        sch.emit("vector", lambda e: e.tensor_tensor(
            wrep[:, 0:64], gth[:, 0:64], sel[:, :], OP.mult), mark="V")
        if DBG:
            vmg2 = sch.cnt["V"]
            sch.wait("sync", "V", vmg2)
            sch.emit("sync", lambda e: e.dma_start(
                out=dbg4[:, 320:384], in_=wrep[:, 0:64]), mark="Q", n=16)
            sch.wait("vector", "Q")
        sch.emit("vector", lambda e: e.tensor_reduce(
            tiny[:, 0:8],
            wrep[:, :].rearrange("p (j c) -> p c j", c=8), AX.X, OP.add),
            mark="V")
        with ES() as phk2:
            mlt2 = phk2.enter_context(nc.sbuf_tensor("mlt2", [128, 4096], F32))
            sqk2 = phk2.enter_context(nc.sbuf_tensor("sqk2", [128, 4096], F32))
            mini_ln2(tiny[:, 0:8], keep2[:, :], ("ln2g", "ln2b"), mlt2, sqk2)
        sch.emit("vector", lambda e: e.tensor_scalar_mul(
            keep2[:, :], keep2[:, :], msk[:, 0:1]), mark="V")
        if DBG:
            vm = sch.cnt["V"]
            sch.wait("sync", "V", vm)
            sch.emit("sync", lambda e: e.dma_start(
                out=dbg5[:, 16:24], in_=keep2[:, :]), mark="Q", n=16)
            sch.emit("sync", lambda e: e.dma_start(
                out=dbg5[:, 24:32], in_=tiny[:, 0:8]), mark="Q", n=16)
            sch.wait("vector", "Q")
        sch.barrier()

        if UPTO == "ag2":
            return finish()

        with ES() as phD:
            eD = phD.enter_context
            xk2h = eD(nc.sbuf_tensor("xk2h", [128, 16384], BF16))
            xk2l = eD(nc.sbuf_tensor("xk2l", [128, 16384], BF16))
            with ES() as ph:
                e3 = ph.enter_context
                fr_hi = e3(nc.sbuf_tensor("fr_hi", [128, 8192], BF16))
                fr_lo = e3(nc.sbuf_tensor("fr_lo", [128, 8192], BF16))
                wstg3v = e3(nc.sbuf_tensor("wstg3", [128, 4096], F32))
                xmtD = e3(nc.sbuf_tensor("xmt3", [128, 4096], F32))
                mx32D = e3(nc.sbuf_tensor("mx32b", [128, 4096], F32))
                sqbD = mx32D
                xct = e3(nc.sbuf_tensor("xct", [128, 4104], F32))
                u2t = xmtD
                mxhD = e3(nc.sbuf_tensor("mxhb", [128, 4096], BF16))
                mxlD = e3(nc.sbuf_tensor("mxlb", [128, 4096], BF16))
                s2t = e3(nc.sbuf_tensor("s2t", [128, 4096], F32))

                load_weight_pair(fwrT, fr_hi, fr_lo, wstg3v, 8192)

                for t in range(NT):
                    sch.wait("sync", "V")
                    sch.wait("sync", "S")
                    sch.emit("sync", lambda e, t=t: e.dma_start(
                        out=xmtD[:, :], in_=ldq(sp_xmid[:, 512 * t:512 * (t + 1)])),
                        mark="Q", n=16)
                    sch.wait("vector", "Q")
                    sch.wait("scalar", "Q")
                    ln_rows(lambda q: xmtD[:, q * TW:(q + 1) * TW], sqbD)
                    for q in range(NQ):
                        sch.emit("vector", lambda e, q=q: e.tensor_copy(
                            xct[:, q * 513:q * 513 + 1], keep2[:, q:q + 1]), mark="V")
                        ln_apply(xmtD[:, q * TW:(q + 1) * TW],
                                 xct[:, q * 513 + 1:q * 513 + 513], q,
                                 affine=("ln2g", "ln2b"))
                    for q in range(NQ):
                        sch.emit("vector", lambda e, q=q: e.tensor_copy(
                            keep2[:, q:q + 1], xct[:, q * 513 + 512:q * 513 + 513]),
                            mark="V")
                    for q in range(NQ):
                        sch.emit("vector", lambda e, q=q: e.tensor_tensor(
                            u2t[:, q * TW:(q + 1) * TW],
                            xct[:, q * 513 + 1:q * 513 + 513],
                            xct[:, q * 513:q * 513 + 512], OP.subtract), mark="V")
                    for q in range(NQ):
                        sch.emit("vector", lambda e, q=q: e.scalar_tensor_tensor(
                            mx32D[:, q * TW:(q + 1) * TW], u2t[:, q * TW:(q + 1) * TW],
                            cvc("cmk", q), xct[:, q * 513:q * 513 + 512],
                            OP.mult, OP.add), mark="V")
                        pair_split(mx32D[:, q * TW:(q + 1) * TW],
                                   xk2h[:, q * 2048 + t * TW:q * 2048 + (t + 1) * TW],
                                   xk2l[:, q * 2048 + t * TW:q * 2048 + (t + 1) * TW])
                    for q in range(NQ):
                        sch.emit("vector", lambda e, q=q: e.scalar_tensor_tensor(
                            mx32D[:, q * TW:(q + 1) * TW], u2t[:, q * TW:(q + 1) * TW],
                            cvc("cmr", q), xct[:, q * 513:q * 513 + 512],
                            OP.mult, OP.add), mark="V")
                        pair_split(mx32D[:, q * TW:(q + 1) * TW],
                                   mxhD[:, q * TW:(q + 1) * TW],
                                   mxlD[:, q * TW:(q + 1) * TW])
                    vm = sch.cnt["V"]
                    sch.wait("tensor", "V", vm)
                    for oq in range(NQ):
                        bank = oq % 2
                        for cq in range(NQ):
                            mm3(bank, wch(fr_hi, cq, oq), wch(fr_lo, cq, oq),
                                mxhD[:, cq * TW:(cq + 1) * TW],
                                mxlD[:, cq * TW:(cq + 1) * TW],
                                start=(cq == 0), stop=(cq == NQ - 1),
                            mark=(cq == NQ - 1))
                        tm = sch.cnt["T"]
                        sch.wait("scalar", "T", tm)
                        sm = sch.emit("scalar", lambda e, oq=oq, bank=bank: e.activation(
                            s2t[:, oq * TW:(oq + 1) * TW], pb[bank][:, :],
                            ACT.Sigmoid), mark="S")
                        sch.consume_bank(bank, "S", sm)
                    sm = sch.cnt["S"]
                    sch.wait("sync", "S", sm)
                    sch.emit("sync", lambda e, t=t: e.dma_start(
                        out=ldq(sp_sig2[:, 512 * t:512 * (t + 1)]), in_=s2t[:, :]),
                        mark="Q", n=16)
                sch.barrier()

            if UPTO == "d":
                return finish()

            with ES() as ph:
                e4 = ph.enter_context
                kvac = e4(nc.sbuf_tensor("kvac", [128, 16384], F32))
                wkst = e4(nc.sbuf_tensor("wkst", [128, 4096], F32))
                wkh = e4(nc.sbuf_tensor("wkh", [128, 4096], BF16))
                wkl = e4(nc.sbuf_tensor("wkl", [128, 4096], BF16))
                wvh = e4(nc.sbuf_tensor("wvh", [128, 4096], BF16))
                wvl = e4(nc.sbuf_tensor("wvl", [128, 4096], BF16))
                r1 = e4(nc.sbuf_tensor("r1", [128, 512], F32))
                k32 = e4(nc.sbuf_tensor("k32", [128, 512], F32))
                kkh = e4(nc.sbuf_tensor("kkh", [128, 2048], BF16))
                kkl = e4(nc.sbuf_tensor("kkl", [128, 2048], BF16))

                for hs in range(8):
                    sch.wait("sync", "V")
                    sch.wait("gpsimd", "V")
                    sch.emit("sync", lambda e, hs=hs: e.dma_start(
                        out=wkst[:, :], in_=ldq(fwkT[:, hs * 512:(hs + 1) * 512])),
                        mark="Q", n=16)
                    sch.emit("gpsimd", lambda e, hs=hs: e.dma_start(
                        out=wkh[:, :], in_=ldq(fwkT[:, hs * 512:(hs + 1) * 512])),
                        mark="G", n=16)
                    sch.wait("vector", "Q")
                    sch.wait("vector", "G")
                    for q in range(NQ):
                        sch.emit("vector", lambda e, q=q: e.tensor_tensor(
                            wkl[:, q * 512:(q + 1) * 512],
                            wkst[:, q * 512:(q + 1) * 512],
                            wkh[:, q * 512:(q + 1) * 512], OP.subtract), mark="V")
                    vm = sch.cnt["V"]
                    sch.wait("sync", "V", vm)
                    sch.wait("gpsimd", "V", vm)
                    sch.emit("sync", lambda e, hs=hs: e.dma_start(
                        out=wkst[:, :], in_=ldq(fwvT[hs * 512:(hs + 1) * 512, :])),
                        mark="Q", n=16)
                    sch.emit("gpsimd", lambda e, hs=hs: e.dma_start(
                        out=wvh[:, :], in_=ldq(fwvT[hs * 512:(hs + 1) * 512, :])),
                        mark="G", n=16)
                    sch.wait("vector", "Q")
                    sch.wait("vector", "G")
                    for hc in range(4):
                        sch.emit("vector", lambda e, hc=hc: e.tensor_tensor(
                            wvl[:, hc * 1024:(hc + 1) * 1024],
                            wkst[:, hc * 1024:(hc + 1) * 1024],
                            wvh[:, hc * 1024:(hc + 1) * 1024], OP.subtract), mark="V")
                    vm = sch.cnt["V"]
                    sch.wait("tensor", "V", vm)

                    for t in range(NT):
                        for hc in range(4):
                            bank = 2 + hc % 2
                            for cq in range(NQ):
                                mm3(bank,
                                    wkh[:, cq * 512 + hc * 128:cq * 512 + (hc + 1) * 128],
                                    wkl[:, cq * 512 + hc * 128:cq * 512 + (hc + 1) * 128],
                                    xk2h[:, cq * 2048 + t * TW:cq * 2048 + (t + 1) * TW],
                                    xk2l[:, cq * 2048 + t * TW:cq * 2048 + (t + 1) * TW],
                                    start=(cq == 0), stop=(cq == NQ - 1),
                                mark=(cq == NQ - 1))
                            tm = sch.cnt["T"]
                            sch.wait("vector", "T", tm)
                            vv = sch.emit("vector", lambda e, bank=bank:
                                          e.tensor_scalar_max(
                                              r1[:, :], pb[bank][:, :], 0.0),
                                          mark="V")
                            sch.consume_bank(bank, "V", vv)
                            sch.emit("vector", lambda e: e.tensor_tensor(
                                k32[:, :], r1[:, :], r1[:, :], OP.mult), mark="V")
                            pair_split(k32[:, :], kkh[:, hc * 512:(hc + 1) * 512],
                                       kkl[:, hc * 512:(hc + 1) * 512])
                        vm = sch.cnt["V"]
                        sch.wait("tensor", "V", vm)
                        for cq in range(NQ):
                            bank = 4 + cq % 2
                            for hc in range(4):
                                mm3(bank,
                                    wvh[:, hc * 1024 + cq * 128:hc * 1024 + (cq + 1) * 128],
                                    wvl[:, hc * 1024 + cq * 128:hc * 1024 + (cq + 1) * 128],
                                    kkh[:, hc * 512:(hc + 1) * 512],
                                    kkl[:, hc * 512:(hc + 1) * 512],
                                    start=(hc == 0), stop=(hc == 3),
                                mark=(hc == 3))
                            tm = sch.cnt["T"]
                            sch.wait("vector", "T", tm)
                            if hs == 0:
                                vv = sch.emit("vector", lambda e, cq=cq, t=t, bank=bank: e.tensor_copy(
                                    kvac[:, cq * 2048 + t * TW:cq * 2048 + (t + 1) * TW],
                                    pb[bank][:, :]), mark="V")
                            else:
                                vv = sch.emit("vector", lambda e, cq=cq, t=t, bank=bank: e.tensor_tensor(
                                    kvac[:, cq * 2048 + t * TW:cq * 2048 + (t + 1) * TW],
                                    kvac[:, cq * 2048 + t * TW:cq * 2048 + (t + 1) * TW],
                                    pb[bank][:, :], OP.add), mark="V")
                            sch.consume_bank(bank, "V", vv)
                sch.barrier()
                vm = sch.cnt["V"]
                sch.wait("sync", "V", vm)
                sch.emit("sync", lambda e: e.dma_start(
                    out=ldq(sp_kv[:, :]), in_=kvac[:, :]), mark="Q", n=16)
            sch.barrier()

        if UPTO == "e":
            return finish()

        with ES() as ph:
            e5 = ph.enter_context
            f_kv = e5(nc.sbuf_tensor("f_kv", [128, 4096], F32))
            f_s2 = e5(nc.sbuf_tensor("f_s2", [128, 4096], F32))
            f_xm = e5(nc.sbuf_tensor("f_xm", [128, 4096], F32))
            f_x = e5(nc.sbuf_tensor("f_x", [128, 4096], F32))
            f_sq = e5(nc.sbuf_tensor("f_sq", [128, 4096], F32))
            f_ao = e5(nc.sbuf_tensor("f_ao", [128, 4096], F32))
            for t in range(NT):
                sch.wait("sync", "V")
                sch.wait("sync", "S")
                for (buf, sp) in ((f_kv, sp_kv), (f_s2, sp_sig2), (f_xm, sp_xmid)):
                    sch.emit("sync", lambda e, buf=buf, sp=sp, t=t: e.dma_start(
                        out=buf[:, :], in_=ldq(sp[:, 512 * t:512 * (t + 1)])),
                        mark="Q", n=16)
                sch.emit("sync", lambda e, t=t: e.dma_start(
                    out=f_x[:, :], in_=ldq(x_fm[:, 512 * t + 1:512 * t + 513])),
                    mark="Q", n=16)
                sch.wait("vector", "Q")
                sch.wait("scalar", "Q")
                for q in range(NQ):
                    sch.emit("vector", lambda e, q=q: e.tensor_tensor(
                        f_ao[:, q * TW:(q + 1) * TW], f_s2[:, q * TW:(q + 1) * TW],
                        f_kv[:, q * TW:(q + 1) * TW], OP.mult), mark="V")
                    sch.emit("vector", lambda e, q=q: e.tensor_tensor(
                        f_ao[:, q * TW:(q + 1) * TW], f_ao[:, q * TW:(q + 1) * TW],
                        f_xm[:, q * TW:(q + 1) * TW], OP.add), mark="V")
                ln_rows(lambda q: f_ao[:, q * TW:(q + 1) * TW], f_sq)
                for q in range(NQ):
                    ln_apply(f_ao[:, q * TW:(q + 1) * TW],
                             f_ao[:, q * TW:(q + 1) * TW], q,
                             add=f_x[:, q * TW:(q + 1) * TW])
                vm = sch.cnt["V"]
                sch.wait("sync", "V", vm)
                sch.emit("sync", lambda e, t=t: e.dma_start(
                    out=ldq(sp_x1[:, 512 * t:512 * (t + 1)]), in_=f_ao[:, :]),
                    mark="Q", n=16)
            sch.barrier()

        if UPTO == "f":
            return finish()

        with ES() as ph:
            e6 = ph.enter_context
            g_x1 = e6(nc.sbuf_tensor("g_x1", [128, 4096], F32))
            g_sq = e6(nc.sbuf_tensor("g_sq", [128, 4096], F32))
            g_z = e6(nc.sbuf_tensor("g_z", [128, 4096], F32))
            g_zh = e6(nc.sbuf_tensor("g_zh", [128, 4096], BF16))
            g_zl = e6(nc.sbuf_tensor("g_zl", [128, 4096], BF16))
            g_th = e6(nc.sbuf_tensor("g_th", [128, 1024], BF16))
            g_tl = e6(nc.sbuf_tensor("g_tl", [128, 1024], BF16))
            for t in range(NT):
                sch.wait("sync", "V")
                sch.wait("sync", "S")
                sch.emit("sync", lambda e, t=t: e.dma_start(
                    out=g_x1[:, :], in_=ldq(sp_x1[:, 512 * t:512 * (t + 1)])),
                    mark="Q", n=16)
                sch.wait("vector", "Q")
                sch.wait("scalar", "Q")
                ln_rows(lambda q: g_x1[:, q * TW:(q + 1) * TW], g_sq)
                for q in range(NQ):
                    ln_apply(g_x1[:, q * TW:(q + 1) * TW],
                             g_z[:, q * TW:(q + 1) * TW], q)
                for q in range(NQ):
                    pair_split(g_z[:, q * TW:(q + 1) * TW],
                               g_zh[:, q * TW:(q + 1) * TW],
                               g_zl[:, q * TW:(q + 1) * TW])
                vm = sch.cnt["V"]
                sch.wait("sync", "V", vm)
                sch.emit("sync", lambda e, t=t: e.dma_start(
                    out=ldq(sp_zfh[:, 512 * t:512 * (t + 1)]), in_=g_zh[:, :]),
                    mark="Q", n=16)
                sch.emit("sync", lambda e, t=t: e.dma_start(
                    out=ldq(sp_zfl[:, 512 * t:512 * (t + 1)]), in_=g_zl[:, :]),
                    mark="Q", n=16)
                vm = sch.cnt["V"]
                sch.wait("tensor", "V", vm)
                for tau in range(4):
                    for q in range(NQ):
                        bank = q % 2
                        sch.claim_bank(bank)
                        sch.emit("tensor", lambda e, tau=tau, q=q, bank=bank: e.transpose(
                            pb[bank][:, 0:128],
                            g_z[:, q * TW + tau * 128:q * TW + (tau + 1) * 128],
                            idt[:, :]), mark="T")
                        tm = sch.cnt["T"]
                        sch.wait("scalar", "T", tm)
                        sm = sch.emit("scalar", lambda e, q=q, bank=bank: e.copy(
                            g_th[:, q * 128:(q + 1) * 128], pb[bank][:, 0:128]),
                            mark="S")
                        sch.wait("vector", "S", sm)
                        vv = sch.emit("vector", lambda e, q=q, bank=bank: e.tensor_tensor(
                            g_tl[:, q * 128:(q + 1) * 128], pb[bank][:, 0:128],
                            g_th[:, q * 128:(q + 1) * 128], OP.subtract), mark="V")
                        sch.consume_bank(bank, "V", vv)
                    vv = sch.cnt["V"]
                    sm = sch.cnt["S"]
                    sch.wait("sync", "V", vv)
                    sch.wait("sync", "S", sm)
                    r0 = t * 512 + tau * 128
                    sch.emit("sync", lambda e, r0=r0: e.dma_start(
                        out=sp_zth[r0:r0 + 128, :], in_=g_th[:, :]), mark="Q", n=16)
                    sch.emit("sync", lambda e, r0=r0: e.dma_start(
                        out=sp_ztl[r0:r0 + 128, :], in_=g_tl[:, :]), mark="Q", n=16)
                    sch.wait("scalar", "Q")
                    sch.wait("vector", "Q")
            sch.barrier()

        if UPTO == "g0":
            return finish()

        with ES() as ph:
            e7 = ph.enter_context
            phh = e7(nc.sbuf_tensor("phh", [128, 16384], BF16))
            phl = e7(nc.sbuf_tensor("phl", [128, 16384], BF16))
            pstg = e7(nc.sbuf_tensor("pstg", [128, 16384], F32))
            load_weight_pair(phiT, phh, phl, pstg, 16384)
            xacc = pstg
            dgh = e7(nc.sbuf_tensor("dgh", [128, 4096], BF16))
            dgl = e7(nc.sbuf_tensor("dgl", [128, 4096], BF16))
            zth = e7(nc.sbuf_tensor("zth", [128, 2048], BF16))
            ztl = e7(nc.sbuf_tensor("ztl", [128, 2048], BF16))
            zfh = e7(nc.sbuf_tensor("zfh2", [128, 1024], BF16))
            zfl = e7(nc.sbuf_tensor("zfl2", [128, 1024], BF16))
            lgt = e7(nc.sbuf_tensor("lgt", [128, 2048], F32))
            ebf = e7(nc.sbuf_tensor("ebf", [128, 2048], F32))
            c32 = lgt
            sgh = e7(nc.sbuf_tensor("sgh", [128, 2048], BF16))
            sgl = e7(nc.sbuf_tensor("sgl", [128, 2048], BF16))
            pbb = e7(nc.sbuf_tensor("pbb", [128, 2048], F32))
            smx = e7(nc.sbuf_tensor("smx", [128, 128], F32))

            sch.emit("sync", lambda e: e.dma_start(
                out=pbb[:, :], in_=phib[0:1, :].broadcast_to([128, E])), mark="Q", n=16)
            sch.wait("vector", "Q")

            for g in range(8):
                for ti in range(2):
                    tau = g * 2 + ti
                    sch.wait("sync", "T")
                    sch.emit("sync", lambda e, tau=tau: e.dma_start(
                        out=zfh[:, :],
                        in_=ldq(sp_zfh[:, tau * 128:(tau + 1) * 128])), mark="Q", n=16)
                    sch.emit("sync", lambda e, tau=tau: e.dma_start(
                        out=zfl[:, :],
                        in_=ldq(sp_zfl[:, tau * 128:(tau + 1) * 128])), mark="Q", n=16)
                    sch.wait("tensor", "Q")
                    for et in range(4):
                        bank = et % 2
                        for q in range(NQ):
                            mm3(bank,
                                zfh[:, q * 128:(q + 1) * 128],
                                zfl[:, q * 128:(q + 1) * 128],
                                phh[:, q * 2048 + et * 512:q * 2048 + (et + 1) * 512],
                                phl[:, q * 2048 + et * 512:q * 2048 + (et + 1) * 512],
                                start=(q == 0), stop=(q == NQ - 1),
                            mark=(q == NQ - 1))
                        tm = sch.cnt["T"]
                        sch.wait("vector", "T", tm)
                        vv = sch.emit("vector", lambda e, et=et, bank=bank: e.tensor_tensor(
                            lgt[:, et * 512:(et + 1) * 512], pb[bank][:, :],
                            pbb[:, et * 512:(et + 1) * 512], OP.add), mark="V")
                        sch.consume_bank(bank, "V", vv)
                    sch.emit("vector", lambda e: e.tensor_reduce(
                        smx[:, 64:128], lgt[:, :].rearrange("p (n q) -> p n q", q=32),
                        AX.X, OP.max), mark="V")
                    sch.emit("vector", lambda e: e.tensor_tensor(
                        ebf[:, :].rearrange("p (n q) -> p n q", q=32),
                        lgt[:, :].rearrange("p (n q) -> p n q", q=32),
                        smx[:, 64:128].unsqueeze(2).broadcast_to([128, 64, 32]),
                        OP.subtract), mark="V")
                    vflush()
                    vm = sch.cnt["V"]
                    sch.wait("scalar", "V", vm)
                    sch.emit("scalar", lambda e: e.activation(
                        ebf[:, :], ebf[:, :], ACT.Exp), mark="S")
                    sflush()
                    sm = sch.cnt["S"]
                    sch.wait("vector", "S", sm)
                    sch.emit("vector", lambda e: e.tensor_reduce(
                        smx[:, 64:128], ebf[:, :].rearrange("p (n q) -> p n q", q=32),
                        AX.X, OP.add), mark="V")
                    sch.emit("vector", lambda e: e.reciprocal(
                        smx[:, 64:128], smx[:, 64:128]), mark="V")
                    sch.emit("vector", lambda e: e.tensor_reduce(
                        smx[:, 0:32], lgt[:, :].rearrange("p (n q) -> p q n", q=32),
                        AX.X, OP.max), mark="V")
                    sch.emit("vector", lambda e: e.tensor_tensor(
                        lgt[:, :].rearrange("p (n q) -> p n q", q=32),
                        lgt[:, :].rearrange("p (n q) -> p n q", q=32),
                        smx[:, 0:32].unsqueeze(1).broadcast_to([128, 64, 32]),
                        OP.subtract), mark="V")
                    vflush()
                    vm = sch.cnt["V"]
                    sch.wait("scalar", "V", vm)
                    sch.emit("scalar", lambda e: e.activation(
                        lgt[:, :], lgt[:, :], ACT.Exp), mark="S")
                    sflush()
                    sm = sch.cnt["S"]
                    sch.wait("vector", "S", sm)
                    sch.emit("vector", lambda e: e.tensor_reduce(
                        smx[:, 32:64], lgt[:, :].rearrange("p (n q) -> p q n", q=32),
                        AX.X, OP.add), mark="V")
                    sch.emit("vector", lambda e: e.reciprocal(
                        smx[:, 0:64], smx[:, 0:64]), mark="V")
                    vflush()
                    sch.emit("vector", lambda e: e.tensor_tensor(
                        c32[:, :].rearrange("p (n q) -> p n q", q=32),
                        c32[:, :].rearrange("p (n q) -> p n q", q=32),
                        smx[:, 32:64].unsqueeze(1).broadcast_to([128, 64, 32]),
                        OP.mult), mark="V")
                    for s in range(16):
                        bank = 2 + s % 2
                        vm = sch.cnt["V"]
                        sch.wait("tensor", "V", vm)
                        sch.claim_bank(bank)
                        sch.emit("tensor", lambda e, s=s, bank=bank: e.transpose(
                            pb[bank][:, 0:128], c32[:, s * 128:(s + 1) * 128],
                            idt[:, :]), mark="T")
                        tm = sch.cnt["T"]
                        sch.wait("vector", "T", tm)
                        sch.emit("vector", lambda e, s=s, bank=bank: e.tensor_copy(
                            sgh[:, s * 128:(s + 1) * 128], pb[bank][0:128, 0:128]),
                            mark="V")
                        vv = sch.emit("vector", lambda e, s=s, bank=bank: e.tensor_tensor(
                            sgl[:, s * 128:(s + 1) * 128], pb[bank][:, 0:128],
                            sgh[:, s * 128:(s + 1) * 128], OP.subtract), mark="V")
                        sch.consume_bank(bank, "V", vv)
                    if DBG and tau == 0:
                        vmc = sch.cnt["V"]
                        sch.wait("sync", "V", vmc)
                        sch.emit("sync", lambda e: e.dma_start(
                            out=dbg4[:, 4096:6144], in_=c32[:, :]), mark="Q", n=16)
                        sch.emit("sync", lambda e: e.dma_start(
                            out=dbg4[:, 6144:7168],
                            in_=sgh[:, :].bitcast(F32)), mark="Q", n=16)
                        sch.emit("sync", lambda e: e.dma_start(
                            out=dbg4[:, 7168:8192],
                            in_=sgl[:, :].bitcast(F32)), mark="Q", n=16)
                        sch.emit("sync", lambda e: e.dma_start(
                            out=dbg4[:, 8192:8224], in_=smx[:, 32:64]),
                            mark="Q", n=16)
                        sch.emit("sync", lambda e: e.dma_start(
                            out=dbg4[:, 8224:8256], in_=smx[:, 0:32]),
                            mark="Q", n=16)
                        sch.wait("vector", "Q")
                    sch.emit("vector", lambda e: e.tensor_tensor(
                        c32[:, :].rearrange("p (n q) -> p n q", q=32),
                        ebf[:, :].rearrange("p (n q) -> p n q", q=32),
                        smx[:, 64:128].unsqueeze(2).broadcast_to([128, 64, 32]),
                        OP.mult), mark="V")
                    pair_split(c32[:, :], dgh[:, ti * 2048:(ti + 1) * 2048],
                               dgl[:, ti * 2048:(ti + 1) * 2048])
                    vv = sch.cnt["V"]
                    sm = sch.cnt["S"]
                    sch.wait("sync", "V", vv)
                    sch.wait("sync", "S", sm)
                    sch.emit("sync", lambda e, tau=tau: e.dma_start(
                        out=sp_csh[:, tau * 128:(tau + 1) * 128].rearrange(
                            "(s p) w -> p s w", p=128), in_=sgh[:, :]),
                        mark="Q", n=16)
                    sch.emit("sync", lambda e, tau=tau: e.dma_start(
                        out=sp_csl[:, tau * 128:(tau + 1) * 128].rearrange(
                            "(s p) w -> p s w", p=128), in_=sgl[:, :]),
                        mark="Q", n=16)
                    sch.emit("sync", lambda e, tau=tau, ti=ti: e.dma_start(
                        out=zth[:, ti * 1024:(ti + 1) * 1024],
                        in_=sp_zth[tau * 128:(tau + 1) * 128, :]), mark="Q", n=16)
                    sch.emit("sync", lambda e, tau=tau, ti=ti: e.dma_start(
                        out=ztl[:, ti * 1024:(ti + 1) * 1024],
                        in_=sp_ztl[tau * 128:(tau + 1) * 128, :]), mark="Q", n=16)
                    sch.wait("scalar", "Q")
                    sch.wait("vector", "Q")
                sch.wait("tensor", "Q")
                sch.wait("tensor", "V")
                for s in range(16):
                    for dt in range(2):
                        bank = 4 + (s + dt) % 2
                        for ti in range(2):
                            mm3(bank,
                                dgh[:, ti * 2048 + s * 128:ti * 2048 + (s + 1) * 128],
                                dgl[:, ti * 2048 + s * 128:ti * 2048 + (s + 1) * 128],
                                zth[:, ti * 1024 + dt * 512:ti * 1024 + (dt + 1) * 512],
                                ztl[:, ti * 1024 + dt * 512:ti * 1024 + (dt + 1) * 512],
                                start=(ti == 0), stop=(ti == 1),
                            mark=(ti == 1))
                        tm = sch.cnt["T"]
                        sch.wait("vector", "T", tm)
                        if g == 0:
                            vv = sch.emit("vector", lambda e, s=s, dt=dt, bank=bank: e.tensor_copy(
                                xacc[:, s * 1024 + dt * 512:s * 1024 + (dt + 1) * 512],
                                pb[bank][:, :]), mark="V")
                        else:
                            vv = sch.emit("vector", lambda e, s=s, dt=dt, bank=bank: e.tensor_tensor(
                                xacc[:, s * 1024 + dt * 512:s * 1024 + (dt + 1) * 512],
                                xacc[:, s * 1024 + dt * 512:s * 1024 + (dt + 1) * 512],
                                pb[bank][:, :], OP.add), mark="V")
                        sch.consume_bank(bank, "V", vv)
            sch.barrier()
            sch.emit("sync", lambda e: e.dma_start(
                out=ccx_in.ap().rearrange("(s p) d -> p s d", p=128),
                in_=xacc[:, :]), mark="Q", n=16)
            sch.barrier()

        if UPTO == "g12":
            return finish()

        with ES() as ph:
            e8 = ph.enter_context
            ystg = e8(nc.sbuf_tensor("ystg", [128, 16384], F32))
            ewh = e8(nc.sbuf_tensor("ewh", [128, 8192], BF16))
            ewl = e8(nc.sbuf_tensor("ewl", [128, 8192], BF16))
            xth = e8(nc.sbuf_tensor("xth", [128, 2048], BF16))
            xtl = e8(nc.sbuf_tensor("xtl", [128, 2048], BF16))
            ybf = e8(nc.sbuf_tensor("ybf", [128, 1024], F32))
            bbf = e8(nc.sbuf_tensor("bbf", [128, 512], F32))
            smx3 = e8(nc.sbuf_tensor("smx2", [128, 16], F32))

            sch.wait("gpsimd", "Q")
            sch.emit("gpsimd", lambda e: e.collective_compute(
                "ReduceScatter", OP.add, replica_groups=[list(range(NCORES))],
                ins=[ccx_in.ap().opt()], outs=[ccx_out.ap().opt()]), mark="CC")
            sch.wait("sync", "CC")
            sch.emit("sync", lambda e: e.dma_start(
                out=ystg[:, 0:2048],
                in_=ccx_out.ap().rearrange("(s p) d -> p s d", p=128)),
                mark="Q", n=16)
            sch.wait("vector", "Q")
            for s in range(2):
                sch.emit("vector", lambda e, s=s: e.tensor_scalar(
                    ystg[:, s * 1024:(s + 1) * 1024], ystg[:, s * 1024:(s + 1) * 1024],
                    -33000.0, 65000.0, OP.max, OP.min), mark="V")
                sch.emit("vector", lambda e, s=s: e.reduce_sum(
                    smx3[:, 0:1], ystg[:, s * 1024:(s + 1) * 1024], AX.X), mark="V")
                sch.emit("vector", lambda e, s=s: e.scalar_tensor_tensor(
                    ystg[:, 2048:3072],
                    smx3[:, 0:1].broadcast_to([128, 1024]), -1.0 / 1024.0,
                    ystg[:, s * 1024:(s + 1) * 1024], OP.mult, OP.add), mark="V")
                sch.emit("vector", lambda e: e.tensor_tensor(
                    ystg[:, 3072:4096], ystg[:, 2048:3072], ystg[:, 2048:3072],
                    OP.mult), mark="V")
                sch.emit("vector", lambda e: e.reduce_sum(
                    smx3[:, 1:2], ystg[:, 3072:4096], AX.X), mark="V")
                vm = sch.cnt["V"]
                sch.wait("sync", "V", vm)
                sch.emit("sync", lambda e: e.dma_start(
                    out=sp_b2, in_=smx3[:, 0:8]), mark="Q", n=16)
                sch.wait("sync", "Q")
                sch.emit("sync", lambda e: e.dma_start(
                    out=smx3[:, 8:16], in_=sp_b2), mark="Q", n=16)
                sch.wait("scalar", "Q")
                sch.emit("scalar", lambda e: e.activation(
                    smx3[:, 2:3], smx3[:, 9:10], ACT.Ln, scale=1.0 / 1024.0,
                    bias=epsb[:, 0:1]), mark="S")
                sm = sch.emit("scalar", lambda e: e.activation(
                    smx3[:, 3:4], smx3[:, 2:3], ACT.Exp, scale=-0.5), mark="S")
                sch.wait("sync", "S", sm)
                sch.emit("sync", lambda e: e.dma_start(
                    out=sp_b2, in_=smx3[:, 0:8]), mark="Q", n=16)
                sch.wait("sync", "Q")
                sch.emit("sync", lambda e: e.dma_start(
                    out=smx3[:, 8:16], in_=sp_b2), mark="Q", n=16)
                sch.wait("vector", "Q")
                sch.emit("vector", lambda e, s=s: e.tensor_scalar_mul(
                    ystg[:, 2048:3072], ystg[:, 2048:3072], smx3[:, 11:12]), mark="V")
                vflush()
                vm = sch.cnt["V"]
                sch.wait("scalar", "V", vm)
                sch.emit("scalar", lambda e, s=s: e.activation(
                    ystg[:, s * 1024:(s + 1) * 1024], ystg[:, 2048:3072],
                    ACT.Tanh), mark="S")
                sflush()
                sm = sch.cnt["S"]
                sch.wait("vector", "S", sm)
                sch.wait("tensor", "S", sm)
            for s in range(2):
                for q in range(NQ):
                    bank = q % 2
                    sch.claim_bank(bank)
                    sch.emit("tensor", lambda e, s=s, q=q, bank=bank: e.transpose(
                        pb[bank][:, 0:128],
                        ystg[:, s * 1024 + q * 128:s * 1024 + (q + 1) * 128],
                        idt[:, :]), mark="T")
                    tm = sch.cnt["T"]
                    sch.wait("vector", "T", tm)
                    sch.emit("vector", lambda e, s=s, q=q, bank=bank: e.tensor_copy(
                        xth[:, q * 256 + s * 128:q * 256 + (s + 1) * 128],
                        pb[bank][0:128, 0:128]), mark="V")
                    vv = sch.emit("vector", lambda e, s=s, q=q, bank=bank: e.tensor_tensor(
                        xtl[:, q * 256 + s * 128:q * 256 + (s + 1) * 128],
                        pb[bank][0:128, 0:128],
                        xth[:, q * 256 + s * 128:q * 256 + (s + 1) * 128],
                        OP.subtract), mark="V")
                    sch.consume_bank(bank, "V", vv)
            if DBG:
                vmg3 = sch.cnt["V"]
                smg3 = sch.cnt["S"]
                sch.wait("sync", "V", vmg3)
                sch.wait("sync", "S", smg3)
                sch.emit("sync", lambda e: e.dma_start(
                    out=dbg4[:, 1024:2048], in_=ystg[:, 0:1024]), mark="Q", n=16)
                sch.emit("sync", lambda e: e.dma_start(
                    out=dbg4[:, 2048:3072], in_=ystg[:, 1024:2048]), mark="Q", n=16)
                sch.emit("sync", lambda e: e.dma_start(
                    out=dbg4[:, 3072:4096],
                    in_=xth[:, :].bitcast(F32)), mark="Q", n=16)
                sch.wait("vector", "Q")
            for n in range(8):
                sch.wait("sync", "V")
                sch.wait("sync", "T")
                sch.wait("gpsimd", "V")
                sch.wait("gpsimd", "T")
                sch.wait("vector", "Q")
                sch.emit("sync", lambda e, n=n: e.dma_start(
                    out=ystg[:, 8192:16384], in_=ldq(expw[n])), mark="Q", n=16)
                sch.emit("gpsimd", lambda e, n=n: e.dma_start(
                    out=ewh[:, :], in_=ldq(expw[n])), mark="G", n=16)
                sch.wait("vector", "Q")
                sch.wait("vector", "G")
                for q in range(NQ):
                    sch.emit("vector", lambda e, q=q: e.tensor_tensor(
                        ewl[:, q * 1024:(q + 1) * 1024],
                        ystg[:, 8192 + q * 1024:8192 + (q + 1) * 1024],
                        ewh[:, q * 1024:(q + 1) * 1024], OP.subtract), mark="V")
                vm = sch.cnt["V"]
                sch.wait("tensor", "V", vm)
                for ht in range(2):
                    bank = 4 + ht
                    sch.wait("sync", "V")
                    sch.emit("sync", lambda e, n=n, ht=ht: e.dma_start(
                        out=bbf[0:32, :],
                        in_=expb[n:n + 1, ht * 512:(ht + 1) * 512].broadcast_to(
                            [32, 512])), mark="Q", n=16)
                    for q in range(NQ):
                        mm3(bank,
                            xth[:, q * 256 + n * 32:q * 256 + n * 32 + 32],
                            xtl[:, q * 256 + n * 32:q * 256 + n * 32 + 32],
                            ewh[:, q * 1024 + ht * 512:q * 1024 + (ht + 1) * 512],
                            ewl[:, q * 1024 + ht * 512:q * 1024 + (ht + 1) * 512],
                            start=(q == 0), stop=(q == NQ - 1), m=32,
                            mark=(q == NQ - 1))
                    tm = sch.cnt["T"]
                    sch.wait("vector", "T", tm)
                    sch.wait("vector", "Q")
                    vv = sch.emit("vector", lambda e, ht=ht, bank=bank: e.tensor_tensor(
                        ybf[0:32, ht * 512:(ht + 1) * 512],
                        pb[bank][0:32, :], bbf[0:32, :], OP.add), mark="V")
                    sch.consume_bank(bank, "V", vv)
                vm = sch.cnt["V"]
                sch.wait("sync", "V", vm)
                sch.emit("sync", lambda e, n=n: e.dma_start(
                    out=ccy_in.ap()[n * 32:(n + 1) * 32, :], in_=ybf[0:32, :]),
                    mark="Q", n=16)
            sch.wait("gpsimd", "Q")
            sch.emit("gpsimd", lambda e: e.collective_compute(
                "AllGather", OP.bypass, replica_groups=[list(range(NCORES))],
                ins=[ccy_in.ap().opt()], outs=[ccy_out.ap().opt()]), mark="CC")
            sch.barrier()

        if UPTO == "g3":
            return finish()

        with ES() as ph:
            e9 = ph.enter_context
            yth = e9(nc.sbuf_tensor("yth", [128, 16384], BF16))
            ytl = e9(nc.sbuf_tensor("ytl", [128, 16384], BF16))
            ystg4v = e9(nc.sbuf_tensor("ystg4", [128, 16384], F32))
            csh = e9(nc.sbuf_tensor("csh", [128, 8192], BF16))
            csl = e9(nc.sbuf_tensor("csl", [128, 8192], BF16))

            sch.wait("sync", "CC")
            sch.wait("gpsimd", "CC")
            sch.emit("sync", lambda e: e.dma_start(
                out=ystg4v[:, :],
                in_=ccy_out.ap().rearrange("(s p) d -> p s d", p=128)), mark="Q", n=16)
            sch.emit("gpsimd", lambda e: e.dma_start(
                out=yth[:, :],
                in_=ccy_out.ap().rearrange("(s p) d -> p s d", p=128)), mark="G", n=16)
            sch.wait("vector", "Q")
            sch.wait("vector", "G")
            for s in range(16):
                sch.emit("vector", lambda e, s=s: e.tensor_tensor(
                    ytl[:, s * 1024:(s + 1) * 1024], ystg4v[:, s * 1024:(s + 1) * 1024],
                    yth[:, s * 1024:(s + 1) * 1024], OP.subtract), mark="V")

            for t in range(NT):
                vm = sch.cnt["V"]
                sch.wait("sync", "V", vm)
                sch.wait("sync", "T")
                sch.emit("sync", lambda e, t=t: e.dma_start(
                    out=csh[:, :],
                    in_=sp_csh[:, t * 512:(t + 1) * 512].rearrange(
                        "(s p) w -> p s w", p=128)), mark="Q", n=16)
                sch.emit("sync", lambda e, t=t: e.dma_start(
                    out=csl[:, :],
                    in_=sp_csl[:, t * 512:(t + 1) * 512].rearrange(
                        "(s p) w -> p s w", p=128)), mark="Q", n=16)
                sch.emit("sync", lambda e, t=t: e.dma_start(
                    out=ystg4v[:, 4096:8192],
                    in_=ldq(sp_x1[:, 512 * t:512 * (t + 1)])), mark="Q", n=16)
                sch.wait("tensor", "Q")
                sch.wait("vector", "Q")
                for hq in range(NQ):
                    bank = hq % 2
                    for s in range(16):
                        mm3(bank,
                            yth[:, s * 1024 + hq * 128:s * 1024 + (hq + 1) * 128],
                            ytl[:, s * 1024 + hq * 128:s * 1024 + (hq + 1) * 128],
                            csh[:, s * 512:(s + 1) * 512],
                            csl[:, s * 512:(s + 1) * 512],
                            start=(s == 0), stop=(s == 15),
                        mark=(s == 15))
                    tm = sch.cnt["T"]
                    sch.wait("vector", "T", tm)
                    vv = sch.emit("vector", lambda e, hq=hq, bank=bank: e.tensor_copy(
                        ystg4v[:, hq * TW:(hq + 1) * TW], pb[bank][:, :]), mark="V")
                    sch.consume_bank(bank, "V", vv)
                if False and DBG and t == 0:
                    vm = sch.cnt["V"]
                    sch.wait("sync", "V", vm)
                    sch.emit("sync", lambda e: e.dma_start(
                        out=dbg4[:, 0:4096], in_=ystg4v[:, 0:4096]), mark="Q", n=16)
                    sch.emit("sync", lambda e: e.dma_start(
                        out=dbg4[:, 4096:8192],
                        in_=csh[:, :].bitcast(F32)), mark="Q", n=16)
                    sch.emit("sync", lambda e: e.dma_start(
                        out=dbg4[:, 8192:12288],
                        in_=yth[:, 0:8192].bitcast(F32)), mark="Q", n=16)
                    sch.emit("sync", lambda e: e.dma_start(
                        out=dbg4[:, 12288:16384],
                        in_=ystg4v[:, 4096:8192]), mark="Q", n=16)
                ln_rows(lambda q: ystg4v[:, q * TW:(q + 1) * TW],
                        ystg4v[:, 8192:12288])
                for q in range(NQ):
                    ln_apply(ystg4v[:, q * TW:(q + 1) * TW],
                             ystg4v[:, q * TW:(q + 1) * TW], q,
                             add=ystg4v[:, 4096 + q * TW:4096 + (q + 1) * TW])
                vm = sch.cnt["V"]
                sch.wait("sync", "V", vm)
                sch.emit("sync", lambda e, t=t: e.dma_start(
                    out=ldq(out_fm[:, 512 * t:512 * (t + 1)]),
                    in_=ystg4v[:, 0:4096]), mark="Q", n=16)
            if DBG:
                for (dst, srcb) in ((dbgx1, ccx_in), (dbgx2, ccx_out),
                                    (dbgy1, ccy_in), (dbgy2, ccy_out)):
                    sch.emit("sync", lambda e, dst=dst, srcb=srcb: e.dma_start(
                        out=dst, in_=srcb.ap()), mark="Q", n=16)
            sch.barrier()

        block = en(nc.Block())
        block.tensor(lambda e: [f(e) for f in sch.q["tensor"]])
        block.vector(lambda e: [f(e) for f in sch.q["vector"]])
        block.scalar(lambda e: [f(e) for f in sch.q["scalar"]])
        block.sync(lambda e: [f(e) for f in sch.q["sync"]])
        block.gpsimd(lambda e: [f(e) for f in sch.q["gpsimd"]])

    return nc


def kernel(**inputs):
    ii = {k: np.asarray(v) for k, v in inputs.items()}
    x = ii["L0_attn_x"].reshape(16384, 1024)

    def t(a):
        return np.ascontiguousarray(np.asarray(a).T)

    com = {
        "wkT": t(ii["L0_attn_att_Wk"]), "wvT": t(ii["L0_attn_att_Wv"]),
        "wrT": t(ii["L0_attn_att_Wr"]), "woT": t(ii["L0_attn_att_Wo"]),
        "fwkT": t(ii["L0_attn_ffn_Wk"]), "fwvT": t(ii["L0_attn_ffn_Wv"]),
        "fwrT": t(ii["L0_attn_ffn_Wr"]), "phiT": t(ii["L0_ffn_phi_w"]),
        "phib": np.ascontiguousarray(
            np.asarray(ii["L0_ffn_phi_b"], np.float32).reshape(1, 2048)),
        "oc": np.full((128, 1), 1.0 / 1024.0, np.float32),
        "ident": np.eye(128, dtype=np.float32),
    }
    cv = np.empty((128, 8 * len(CV)), np.float32)
    vals = {
        "ln1g": ii["L0_attn_ln1_g"], "ln1b": ii["L0_attn_ln1_b"],
        "ln2g": ii["L0_attn_ln2_g"], "ln2b": ii["L0_attn_ln2_b"],
        "decay": ii["L0_attn_time_decay"], "tfirst": ii["L0_attn_time_first"],
        "tmk": ii["L0_attn_time_mix_k"], "tmv": ii["L0_attn_time_mix_v"],
        "tmr": ii["L0_attn_time_mix_r"], "cmk": ii["L0_attn_cm_k"],
        "cmr": ii["L0_attn_cm_r"],
    }
    for j, name in enumerate(CV):
        cv[:, 8 * j:8 * (j + 1)] = np.asarray(vals[name], np.float32).reshape(8, 128).T
    com["cv"] = cv

    in_maps = []
    for i in range(NCORES):
        m = dict(com)
        lo = 2048 * i
        xs = np.empty((1024, 2049), np.float32)
        xs[:, 1:] = x[lo:lo + 2048].T
        xs[:, 0] = 0.0 if i == 0 else x[lo - 1]
        m["x_fm"] = np.ascontiguousarray(xs)
        m["expw"] = np.ascontiguousarray(ii["L0_ffn_expert_w"][8 * i:8 * (i + 1)])
        m["expb"] = np.ascontiguousarray(ii["L0_ffn_expert_b"][8 * i:8 * (i + 1)])
        cmul = np.full((128, 8), 1e9, np.float32)
        for j in range(i):
            cmul[:, j] = 2048.0 * (i - 1 - j)
        m["cmul"] = cmul
        cmwv = np.full((128, 128), 1e9, np.float32)
        for j in range(i):
            cmwv[:, j * 16:(j + 1) * 16] = 2048.0 * (i - 1 - j)
        m["cmw"] = cmwv
        selp = np.zeros((128, 64), np.float32)
        if i > 0:
            selp[:, 8 * (i - 1):8 * i] = 1.0
        m["selp"] = selp
        m["mask0"] = np.full((128, 1), 0.0 if i == 0 else 1.0, np.float32)
        m["xb"] = np.ascontiguousarray(xs[:, 0].reshape(8, 128).T)
        in_maps.append(m)

    nc = build_program()
    res = run_bass_kernel_spmd(nc, in_maps, core_ids=list(range(NCORES)))
    out = np.empty((16384, 1024), np.float32)
    for i in range(NCORES):
        out[2048 * i:2048 * (i + 1)] = res.results[i]["out_fm"].T
    return out.reshape(1, 16384, 1024)
